# version 01
# baseline (speedup 1.0000x reference)
import numpy as np
import ml_dtypes
import concourse.bass as bass
import concourse.mybir as mybir
from concourse.bass_utils import run_bass_kernel_spmd
from contextlib import ExitStack

F32 = mybir.dt.float32
BF16 = mybir.dt.bfloat16
AF = mybir.ActivationFunctionType
ALU = mybir.AluOpType
AX = mybir.AxisListType

B, L, D = 4, 4096, 2048
H, NKV, E = 16, 4, 128
INTER = 8192
TOK = 2048
NT = TOK // 128
NKT = L // 128
NC = D // 128
QSCALE = float(1.0 / np.sqrt(128.0))
EPS = 1e-5

R0_TABLE = {0: [1024, 1536, 2048, 2560], 1: [0, 512, 3072, 3584]}

DEBUG = False
SERIAL_P0 = False


class Sem:
    def __init__(self, nc, ctx, name):
        self.h = ctx.enter_context(nc.semaphore(name))
        self.n = 0

    def inc(self, inst, k=1):
        inst.then_inc(self.h, k)
        self.n += k
        return self.n


def build(debug=DEBUG):
    nc = bass.Bass("TRN2", target_bir_lowering=False)
    T, V, S, G, Y = nc.tensor, nc.vector, nc.scalar, nc.gpsimd, nc.sync

    dt_ = lambda n, s, d, k: nc.dram_tensor(n, s, d, kind=k).ap()
    x_kv = dt_("x_kv", [L, D], F32, "ExternalInput")
    x_own = dt_("x_own", [TOK, D], F32, "ExternalInput")
    wq = dt_("wq", [4, D, 512], F32, "ExternalInput")
    wk = dt_("wk", [D, 512], F32, "ExternalInput")
    wv = dt_("wv", [D, 512], F32, "ExternalInput")
    wo = dt_("wo", [4, D, 512], F32, "ExternalInput")
    wgu = dt_("wgu", [128, D, 128], F32, "ExternalInput")
    wdn = dt_("wdn", [4, INTER, 512], F32, "ExternalInput")
    bq = dt_("bq", [1, D], F32, "ExternalInput")
    bk = dt_("bk", [1, 512], F32, "ExternalInput")
    bv = dt_("bv", [1, 512], F32, "ExternalInput")
    bo = dt_("bo", [1, D], F32, "ExternalInput")
    gam = dt_("gam", [1, D], F32, "ExternalInput")
    bet = dt_("bet", [1, D], F32, "ExternalInput")
    out = dt_("out", [TOK, D], F32, "ExternalOutput")
    scr_kind = "ExternalOutput" if debug else "Internal"
    q_scr = dt_("q_scr", [4, 128, H, 512], BF16, scr_kind)
    xn_scr = dt_("xn_scr", [4, 128, NC, 512], BF16, scr_kind)
    ht_scr = dt_("ht_scr", [64, 128, TOK], BF16, scr_kind)
    s_scr = dt_("s_scr", [TOK, D], F32, scr_kind)
    if debug:
        d_kt = dt_("d_kt", [128, NKV, L], BF16, "ExternalOutput")
        d_v = dt_("d_v", [128, NKT, 512], BF16, "ExternalOutput")
        d_stat = dt_("d_stat", [128, 2, NT], F32, "ExternalOutput")
        d_xnt = dt_("d_xnt", [8, 128, NC, 512], BF16, "ExternalOutput")
        d_kt2 = dt_("d_kt2", [8, 128, NKV, 512], BF16, "ExternalOutput")
        d_wk = dt_("d_wk", [128, NC, 512], BF16, "ExternalOutput")

    ctx = ExitStack()
    with ctx:
        sbuf = lambda n, s, d: ctx.enter_context(nc.sbuf_tensor(n, s, d))
        sem = lambda n: Sem(nc, ctx, n)
        block = ctx.enter_context(nc.Block())

        mu_all = sbuf("mu_all", [128, NT], F32)
        rs_all = sbuf("rs_all", [128, NT], F32)
        identB = sbuf("identB", [128, 128], BF16)
        onesB = sbuf("onesB", [128, 1], BF16)
        cst = sbuf("cst", [128, 8], F32)
        bn_b = sbuf("bn_b", [128, 8, 3], F32)
        ln_t = sbuf("ln_t", [128, 6], F32)
        sq8 = sbuf("sq8", [128, 8], F32)

        sI = sem("sI")
        G.memset(identB[:], 0.0)
        sI.inc(G.affine_select(out=identB[:], in_=identB[:], compare_op=ALU.not_equal,
                               fill=1.0, base=0, pattern=[[-1, 128]], channel_multiplier=1))
        G.memset(onesB[:], 1.0)
        G.memset(cst[:, 0:1], EPS)
        G.memset(cst[:, 1:2], -0.5)
        G.memset(cst[:, 2:3], 0.125)
        G.memset(cst[:, 3:4], 1.0 / 2048.0)
        sI.inc(G.memset(cst[:, 4:5], QSCALE))
        INIT_N = sI.n

        def emit_ln_stats(xf_ap, mu_out, rstd_out, vsem, ssem):
            for j in range(4):
                V.bn_stats(bn_b[:, 2 * j:2 * j + 2, :], xf_ap[:, j * 512:(j + 1) * 512])
            V.drain()
            V.reduce_sum(ln_t[:, 0:1], bn_b[:, :, 1], axis=AX.X)
            V.reduce_sum(ln_t[:, 1:2], bn_b[:, :, 2], axis=AX.X)
            V.tensor_mul(sq8[:], bn_b[:, :, 1], bn_b[:, :, 1])
            V.drain()
            V.reduce_sum(ln_t[:, 3:4], sq8[:], axis=AX.X)
            V.tensor_scalar_mul(mu_out, ln_t[:, 0:1], 0.125)
            V.drain()
            V.tensor_mul(ln_t[:, 4:5], mu_out, mu_out)
            V.drain()
            V.tensor_scalar(out=ln_t[:, 4:5], in0=ln_t[:, 3:4], scalar1=0.125,
                            scalar2=ln_t[:, 4:5], op0=ALU.mult, op1=ALU.subtract)
            V.drain()
            v_ev = vsem.inc(V.tensor_scalar(out=ln_t[:, 1:2], in0=ln_t[:, 1:2],
                                            scalar1=1.0 / 2048.0, scalar2=ln_t[:, 4:5],
                                            op0=ALU.mult, op1=ALU.add))
            S.wait_ge(vsem.h, v_ev)
            S.activation(out=ln_t[:, 2:3], in_=ln_t[:, 1:2], func=AF.Ln, bias=cst[:, 0:1])
            S.drain()
            s_ev = ssem.inc(S.activation(out=rstd_out, in_=ln_t[:, 2:3], func=AF.Exp,
                                         scale=cst[:, 1:2]))
            return s_ev

        kv_scope = ctx.enter_context(ExitStack())
        KT_sb = kv_scope.enter_context(nc.sbuf_tensor("KT", [128, NKV, L], BF16))
        V_sb = kv_scope.enter_context(nc.sbuf_tensor("Vt", [128, NKT, 512], BF16))

        with nc.sbuf_tensor("xbufA", [128, 3, D], F32) as xbuf, \
             nc.sbuf_tensor("xn_tmA", [128, 2, D], BF16) as xn_tm, \
             nc.sbuf_tensor("xnT_gA", [128, 2, NC, 512], BF16) as xnT_g, \
             nc.sbuf_tensor("wk_sb", [128, NC, 512], BF16) as wk_sb, \
             nc.sbuf_tensor("wv_sb", [128, NC, 512], BF16) as wv_sb, \
             nc.sbuf_tensor("bv_bc", [128, 512], F32) as bv_bc, \
             nc.sbuf_tensor("bk_pp", [128, NKV], F32) as bk_pp, \
             nc.psum_tensor("ptpA", [128, 4, 8, 128], BF16) as ptp, \
             nc.psum_tensor("kps", [128, 2, 512], F32) as kps, \
             nc.psum_tensor("vps", [128, 2, 512], F32) as vps:

            aX = sem("aX"); aR = sem("aR"); aN = sem("aN")
            aT = sem("aT"); aE = sem("aE")
            aKV = sem("aKV"); aKVE = sem("aKVE")
            aW = sem("aW"); aLV = sem("aLV")
            dXN = sem("dXN")

            aW.inc(Y.dma_start(out=bv_bc[:], in_=bv.partition_broadcast(128)), 16)
            aW.inc(Y.dma_start(out=bk_pp[:], in_=bk.rearrange("x (c p) -> (x p) c", p=128),
                               allow_slow_non_contiguous=True), 16)
            for c in range(NC):
                aW.inc(G.dma_start(out=wk_sb[:, c, :], in_=wk[c * 128:(c + 1) * 128, :]), 16)
                aW.inc(G.dma_start(out=wv_sb[:, c, :], in_=wv[c * 128:(c + 1) * 128, :]), 16)
            if debug:
                Y.wait_ge(aW.h, aW.n)
                aW.inc(Y.dma_start(out=d_wk, in_=wk_sb[:]), 16)
            AW_N = aW.n

            x_ev = [0] * NKT; n_ev = [0] * NKT; t_ev = [0] * NKT
            e_ev = [0] * NKT; xf_ev = [0] * NKT
            gu_ev = [0] * 8
            kve_ev = []
            kv_mm = []

            for i in range(NKT):
                g, sl = i // 4, i % 3
                if i >= 3:
                    Y.wait_ge(aN.h, xf_ev[i - 3])
                x_ev[i] = aX.inc(Y.dma_start(out=xbuf[:, sl, :],
                                             in_=x_kv[i * 128:(i + 1) * 128, :]), 16)
                V.wait_ge(aX.h, x_ev[i])
                if i == 0:
                    V.wait_ge(sI.h, INIT_N)
                r_ev = emit_ln_stats(xbuf[:, sl, :], ln_t[:, 0:1], ln_t[:, 2:3], aLV, aR)
                V.wait_ge(aR.h, r_ev)
                if i >= 2:
                    V.wait_ge(aT.h, t_ev[i - 2])
                inst = V.tensor_scalar(out=xn_tm[:, i % 2, :], in0=xbuf[:, sl, :],
                                       scalar1=ln_t[:, 0:1], scalar2=ln_t[:, 2:3],
                                       op0=ALU.subtract, op1=ALU.mult)
                n_ev[i] = aN.inc(inst)
                xf_ev[i] = n_ev[i]
                T.wait_ge(aN.h, n_ev[i])
                if i == 0:
                    T.wait_ge(sI.h, INIT_N)
                if i >= 2:
                    T.wait_ge(aE.h, e_ev[i - 2])
                for c in range(NC):
                    inst = T.transpose(ptp[:, (i % 2) * 2 + c // 8, c % 8, :],
                                       xn_tm[:, i % 2, c * 128:(c + 1) * 128], identB[:])
                t_ev[i] = aT.inc(inst)
                V.wait_ge(aT.h, t_ev[i])
                if g >= 2:
                    V.wait_ge(aKV.h, gu_ev[g - 2])
                    if debug:
                        V.wait_ge(dXN.h, 16 * (g - 1))
                V.tensor_copy(xnT_g[:, g % 2, 0:8, (i % 4) * 128:(i % 4 + 1) * 128],
                              ptp[:, (i % 2) * 2, :, :])
                e_ev[i] = aE.inc(V.tensor_copy(
                    xnT_g[:, g % 2, 8:16, (i % 4) * 128:(i % 4 + 1) * 128],
                    ptp[:, (i % 2) * 2 + 1, :, :]))

                if i % 4 == 3:
                    if debug:
                        Y.wait_ge(aE.h, e_ev[i])
                        dXN.inc(Y.dma_start(out=d_xnt[g], in_=xnT_g[:, g % 2, :, :]), 16)
                    T.wait_ge(aE.h, e_ev[i])
                    if g == 0:
                        T.wait_ge(aW.h, AW_N)
                    for oc in range(NKV):
                        if SERIAL_P0 and kve_ev:
                            T.wait_ge(aKVE.h, kve_ev[-1])
                        elif len(kve_ev) >= 2:
                            T.wait_ge(aKVE.h, kve_ev[-2])
                        for c in range(NC):
                            inst = T.matmul(kps[:, oc % 2, :],
                                            wk_sb[:, c, oc * 128:(oc + 1) * 128],
                                            xnT_g[:, g % 2, c, :],
                                            start=(c == 0), stop=(c == NC - 1))
                        kv_mm.append(aKV.inc(inst))
                        V.wait_ge(aKV.h, kv_mm[-1])
                        kve_ev.append(aKVE.inc(V.tensor_scalar(
                            out=KT_sb[:, oc, g * 512:(g + 1) * 512], in0=kps[:, oc % 2, :],
                            scalar1=bk_pp[:, oc:oc + 1], scalar2=None, op0=ALU.add)))
                    if debug:
                        Y.wait_ge(aKVE.h, kve_ev[-1])
                        dXN.inc(Y.dma_start(out=d_kt2[g], in_=KT_sb[:, :, g * 512:(g + 1) * 512]), 16)
                    for t in range(4):
                        if SERIAL_P0 and kve_ev:
                            T.wait_ge(aKVE.h, kve_ev[-1])
                        elif len(kve_ev) >= 2:
                            T.wait_ge(aKVE.h, kve_ev[-2])
                        for c in range(NC):
                            inst = T.matmul(vps[:, t % 2, :],
                                            xnT_g[:, g % 2, c, t * 128:(t + 1) * 128],
                                            wv_sb[:, c, :],
                                            start=(c == 0), stop=(c == NC - 1))
                        kv_mm.append(aKV.inc(inst))
                        if t == 3:
                            gu_ev[g] = kv_mm[-1]
                        V.wait_ge(aKV.h, kv_mm[-1])
                        kve_ev.append(aKVE.inc(V.scalar_tensor_tensor(
                            out=V_sb[:, g * 4 + t, :], in0=vps[:, t % 2, :], scalar=1.0,
                            in1=bv_bc[:], op0=ALU.mult, op1=ALU.add)))

            P0A_V = aKVE.n
            aKVE_h = aKVE.h
            AKV_F = aKV.n
            aKV_h = aKV.h

            if debug:
                Y.wait_ge(aKVE.h, P0A_V)
                Y.dma_start(out=d_kt, in_=KT_sb[:]).then_inc(sI.h, 16); sI.n += 16
                Y.dma_start(out=d_v, in_=V_sb[:]).then_inc(sI.h, 16); sI.n += 16

        with nc.sbuf_tensor("xbufB", [128, 3, D], F32) as xbuf, \
             nc.sbuf_tensor("xn_tmB", [128, 2, D], BF16) as xn_tm, \
             nc.sbuf_tensor("xnT_gB", [128, 2, NC, 512], BF16) as xnT_g, \
             nc.sbuf_tensor("wq_sb", [128, 4, NC, 512], BF16) as wq_sb, \
             nc.sbuf_tensor("qt_evb", [128, 4, 512], BF16) as qt_evb, \
             nc.sbuf_tensor("bq_pp", [128, NC], F32) as bq_pp, \
             nc.psum_tensor("ptpB", [128, 4, 8, 128], BF16) as ptp, \
             nc.psum_tensor("qps", [128, 4, 512], F32) as qps:

            bX = sem("bX"); bR = sem("bR"); bN = sem("bN")
            bT = sem("bT"); bE = sem("bE")
            bQ = sem("bQ"); bQE = sem("bQE"); bW = sem("bW"); bSP = sem("bSP")
            bLV = sem("bLV")

            Y.wait_ge(aKVE_h, P0A_V)
            Y.wait_ge(aKV_h, AKV_F)
            G.wait_ge(aKVE_h, P0A_V)
            G.wait_ge(aKV_h, AKV_F)
            bW.inc(Y.dma_start(out=bq_pp[:], in_=bq.rearrange("x (c p) -> (x p) c", p=128),
                               allow_slow_non_contiguous=True), 16)
            for osl in range(4):
                for c in range(NC):
                    bW.inc(G.dma_start(out=wq_sb[:, osl, c, :],
                                       in_=wq[osl, c * 128:(c + 1) * 128, :]), 16)
            BW_N = bW.n

            x_ev = [0] * NT; n_ev = [0] * NT; t_ev = [0] * NT
            e_ev = [0] * NT; xf_ev = [0] * NT
            qmm_ev = [0] * 4
            sp_ev = [0] * 4
            qe_ev = []
            spl = []

            for i in range(NT):
                g, sl = i // 4, i % 3
                if i >= 3:
                    Y.wait_ge(bN.h, xf_ev[i - 3])
                x_ev[i] = bX.inc(Y.dma_start(out=xbuf[:, sl, :],
                                             in_=x_own[i * 128:(i + 1) * 128, :]), 16)
                V.wait_ge(bX.h, x_ev[i])
                r_ev = emit_ln_stats(xbuf[:, sl, :], mu_all[:, i:i + 1],
                                     rs_all[:, i:i + 1], bLV, bR)
                V.wait_ge(bR.h, r_ev)
                if i >= 2:
                    V.wait_ge(bT.h, t_ev[i - 2])
                inst = V.tensor_scalar(out=xn_tm[:, i % 2, :], in0=xbuf[:, sl, :],
                                       scalar1=mu_all[:, i:i + 1], scalar2=rs_all[:, i:i + 1],
                                       op0=ALU.subtract, op1=ALU.mult)
                n_ev[i] = bN.inc(inst)
                xf_ev[i] = n_ev[i]
                T.wait_ge(bN.h, n_ev[i])
                if i >= 2:
                    T.wait_ge(bE.h, e_ev[i - 2])
                for c in range(NC):
                    inst = T.transpose(ptp[:, (i % 2) * 2 + c // 8, c % 8, :],
                                       xn_tm[:, i % 2, c * 128:(c + 1) * 128], identB[:])
                t_ev[i] = bT.inc(inst)
                V.wait_ge(bT.h, t_ev[i])
                if g >= 2:
                    V.wait_ge(bSP.h, sp_ev[g - 2])
                    V.wait_ge(bQ.h, qmm_ev[g - 2])
                V.tensor_copy(xnT_g[:, g % 2, 0:8, (i % 4) * 128:(i % 4 + 1) * 128],
                              ptp[:, (i % 2) * 2, :, :])
                e_ev[i] = bE.inc(V.tensor_copy(
                    xnT_g[:, g % 2, 8:16, (i % 4) * 128:(i % 4 + 1) * 128],
                    ptp[:, (i % 2) * 2 + 1, :, :]))

                if i % 4 == 3:
                    Y.wait_ge(bE.h, e_ev[i])
                    Y.dma_start(out=xn_scr[g], in_=xnT_g[:, g % 2, :, :]).then_inc(bSP.h, 16)
                    bSP.n += 16
                    T.wait_ge(bE.h, e_ev[i])
                    if g == 0:
                        T.wait_ge(bW.h, BW_N)
                    for osl in range(4):
                        if qe_ev:
                            T.wait_ge(bQE.h, qe_ev[-1])
                        for oc in range(4):
                            for c in range(NC):
                                inst = T.matmul(qps[:, oc, :],
                                                wq_sb[:, osl, c, oc * 128:(oc + 1) * 128],
                                                xnT_g[:, g % 2, c, :],
                                                start=(c == 0), stop=(c == NC - 1))
                        qmm_ev[g] = bQ.inc(inst)
                        V.wait_ge(bQ.h, qmm_ev[g])
                        if spl:
                            V.wait_ge(bSP.h, spl[-1])
                        for oc in range(4):
                            inst = V.tensor_scalar(out=qt_evb[:, oc, :], in0=qps[:, oc, :],
                                                   scalar1=bq_pp[:, osl * 4 + oc:osl * 4 + oc + 1],
                                                   scalar2=cst[:, 4:5],
                                                   op0=ALU.add, op1=ALU.mult)
                        qe_ev.append(bQE.inc(inst))
                        Y.wait_ge(bQE.h, qe_ev[-1])
                        Y.dma_start(out=q_scr[g][:, osl * 4:(osl + 1) * 4, :],
                                    in_=qt_evb[:]).then_inc(bSP.h, 16)
                        bSP.n += 16
                        spl.append(bSP.n)
                    sp_ev[g] = bSP.n

            P0B_SP = bSP.n
            bSP_h = bSP.h
            if debug:
                Y.wait_ge(bR.h, bR.n)
                Y.dma_start(out=d_stat[:, 0, :], in_=mu_all[:]).then_inc(sI.h, 16); sI.n += 16
                Y.dma_start(out=d_stat[:, 1, :], in_=rs_all[:]).then_inc(sI.h, 16); sI.n += 16

        with ExitStack() as p1s:
            sb1 = lambda n, s, d: p1s.enter_context(nc.sbuf_tensor(n, s, d))
            ps1 = lambda n, s, d: p1s.enter_context(nc.psum_tensor(n, s, d))
            bo_bc = sb1("bo_bc", [128, D], F32)
            gam_bc = sb1("gam_bc", [128, D], F32)
            bet_bc = sb1("bet_bc", [128, D], F32)
            identF = sb1("identF", [128, 128], F32)
            mask_lt = sb1("mask_lt", [128, 128], F32)
            u_row = sb1("u_row", [128, 512], F32)
            qt_sb = sb1("qt_sb", [128, 2, 4, 512], BF16)
            wo_sb = sb1("wo_sb", [128, 2, NC, 512], BF16)
            PT_sb = sb1("PT_sb", [128, 2, 512], BF16)
            u4 = sb1("u4", [128, 4], F32)
            urec = sb1("urec", [128, 4], F32)
            ctxu = sb1("ctxu", [128, 512], BF16)
            ctm = sb1("ctm", [128, 4, 128], BF16)
            ctxT = sb1("ctxT", [128, NC, 512], BF16)
            y4 = sb1("y4", [128, 4, D], F32)
            e2 = sb1("e2", [128, D], F32)
            xot = sb1("xot", [128, D], F32)
            mu2 = sb1("mu2", [128, 1], F32)
            rs2 = sb1("rs2", [128, 1], F32)

            sc_ps = ps1("sc_ps", [128, 2, 512], F32)
            ctx_ps = ps1("ctx_ps", [128, 512], F32)
            u_ps = ps1("u_ps", [128, 512], F32)
            ctp_ps = ps1("ctp_ps", [128, 8, 128], BF16)
            ctf_ps = ps1("ctf_ps", [128, 8, 128], BF16)
            y_ps = ps1("y_ps", [128, 2, 512], F32)

            Y.wait_ge(bSP_h, P0B_SP)
            sI.inc(Y.dma_start(out=bo_bc[:], in_=bo.partition_broadcast(128)), 16)
            sI.inc(Y.dma_start(out=gam_bc[:], in_=gam.partition_broadcast(128)), 16)
            sI.inc(Y.dma_start(out=bet_bc[:], in_=bet.partition_broadcast(128)), 16)
            G.wait_ge(bSP_h, P0B_SP)
            G.memset(identF[:], 0.0)
            sI.inc(G.affine_select(out=identF[:], in_=identF[:], compare_op=ALU.not_equal,
                                   fill=1.0, base=0, pattern=[[-1, 128]],
                                   channel_multiplier=1))
            G.memset(mask_lt[:], 0.0)
            sI.inc(G.affine_select(out=mask_lt[:], in_=mask_lt[:], compare_op=ALU.is_ge,
                                   fill=-1e9, base=0, pattern=[[1, 128]],
                                   channel_multiplier=-1))
            sI.inc(G.memset(u_row[:], 0.0))
            LATE_N = sI.n

            pQT = sem("pQT"); pWO = sem("pWO"); pSC = sem("pSC"); pMK = sem("pMK")
            pPX = sem("pPX"); pCU = sem("pCU"); pCE = sem("pCE"); pUD = sem("pUD")
            pUR = sem("pUR"); pCD = sem("pCD"); pCT = sem("pCT")
            pY = sem("pY"); pEV = sem("pEV"); pGE = sem("pGE")
            pLN = sem("pLN"); pLS = sem("pLS"); pXO = sem("pXO"); pSP1 = sem("pSP1")
            p1_done = sem("p1_done")

            engines = [T, V, S, G, Y]
            p1_sems = [pQT, pWO, pSC, pMK, pPX, pCU, pCE, pUD, pUR, pCD, pCT,
                       pY, pEV, pGE, pLN, pLS, pXO, pSP1]

            def emit_attn(par):
                R0s = R0_TABLE[par]
                for eng in engines:
                    eng.wait_ge(aKVE_h, P0A_V)
                    eng.wait_ge(bSP_h, P0B_SP)
                    eng.wait_ge(sI.h, LATE_N)
                exp_slot = [0, 0]
                cu_slot = [0, 0]
                ev_slot = [0, 0]
                sc_end = [0]
                qtg_ev = {}
                wo_use = [0, 0]

                def load_qtg(q_, hg_, gate):
                    if gate:
                        Y.wait_ge(pSC.h, gate)
                    qtg_ev[(q_, hg_)] = pQT.inc(Y.dma_start(
                        out=qt_sb[:, hg_ % 2, :, :],
                        in_=q_scr[q_][:, hg_ * 4:(hg_ + 1) * 4, :]), 16)

                load_qtg(0, 0, 0)

                def emit_cu(kt, pr, ss, ktmax, ce_base, kv, exp_at):
                    T.wait_ge(pPX.h, exp_at[kt])
                    if kt == 0:
                        T.wait_ge(pCE.h, ce_base)
                    T.matmul(ctx_ps[:, ss:512], V_sb[:, kt, kv * 128:(kv + 1) * 128],
                             PT_sb[:, pr, ss:512], start=(kt == 0), stop=(kt == ktmax - 1))
                    inst = T.matmul(u_ps[0:1, ss:512], onesB[:], PT_sb[:, pr, ss:512],
                                    start=(kt == 0), stop=(kt == ktmax - 1))
                    cu_slot[pr] = pCU.inc(inst)

                for q in range(4):
                    R0 = R0s[q]
                    KT0 = R0 // 128
                    ktmax = KT0 + 4
                    wo_ev = [0, 0, 0, 0]
                    for osl in range(2):
                        G.wait_ge(pY.h, wo_use[osl % 2])
                        for c in range(NC):
                            inst = G.dma_start(out=wo_sb[:, osl % 2, c, :],
                                               in_=wo[osl, c * 128:(c + 1) * 128, :])
                            inst.then_inc(pWO.h, 16)
                            pWO.n += 16
                        wo_ev[osl] = pWO.n
                    for h in range(H):
                        hg = h // 4
                        if h % 4 == 0:
                            nq, nhg = (q, hg + 1) if hg < 3 else (q + 1, 0)
                            if nq < 4 and (q, hg) != (0, 0):
                                load_qtg(nq, nhg, sc_end[0])
                            elif (q, hg) == (0, 0):
                                load_qtg(0, 1, 0)
                        kv = h % NKV
                        exp_at = {}
                        pend = None
                        ce_base = pCE.n
                        for kt in range(ktmax):
                            ss = 128 * max(0, kt - KT0)
                            pr = kt % 2
                            T.wait_ge(pPX.h, exp_slot[pr])
                            if h % 4 == 0 and kt == 0:
                                T.wait_ge(pQT.h, qtg_ev[(q, hg)])
                            inst = T.matmul(sc_ps[:, pr, ss:512],
                                            KT_sb[:, kv, kt * 128:(kt + 1) * 128],
                                            qt_sb[:, hg % 2, h % 4, ss:512],
                                            start=True, stop=True)
                            sc_at = pSC.inc(inst)
                            masked = kt >= KT0
                            if masked:
                                V.wait_ge(pSC.h, sc_at)
                                mk_at = pMK.inc(V.tensor_tensor(
                                    sc_ps[:, pr, ss:ss + 128], sc_ps[:, pr, ss:ss + 128],
                                    mask_lt[:], ALU.add))
                            S.wait_ge(pSC.h, sc_at)
                            if masked:
                                S.wait_ge(pMK.h, mk_at)
                            S.wait_ge(pCU.h, cu_slot[pr])
                            exp_at[kt] = pPX.inc(S.activation(
                                out=PT_sb[:, pr, ss:512], in_=sc_ps[:, pr, ss:512],
                                func=AF.Exp))
                            exp_slot[pr] = exp_at[kt]
                            if pend is not None:
                                emit_cu(*pend, ktmax, ce_base, kv, exp_at)
                            pend = (kt, pr, ss)
                        emit_cu(*pend, ktmax, ce_base, kv, exp_at)
                        cu_at = pCU.n
                        sc_end[0] = pSC.n
                        V.wait_ge(pCU.h, cu_at)
                        V.tensor_copy(u_row[0:1, :], u_ps[0:1, :])
                        ce1 = pCE.inc(V.tensor_copy(ctxu[:], ctx_ps[:]))
                        T.wait_ge(pCE.h, ce1)
                        T.wait_ge(pUR.h, pUR.n)
                        if h == 0:
                            T.wait_ge(pEV.h, pEV.n)
                        for t in range(4):
                            inst = T.transpose(y_ps[:, 0, t * 128:(t + 1) * 128],
                                               u_row[:, t * 128:(t + 1) * 128], identF[:])
                        ud = pUD.inc(inst)
                        T.wait_ge(pCD.h, pCD.n)
                        for t in range(4):
                            inst = T.transpose(ctp_ps[:, t, :],
                                               ctxu[:, t * 128:(t + 1) * 128], identB[:])
                        cdt = pUD.inc(inst)
                        V.wait_ge(pUD.h, ud)
                        V.tensor_copy(u4[:], y_ps[:, 0, 0:512].rearrange(
                            "p (a b) -> p a b", b=128)[:, :, 0])
                        V.drain()
                        pUR.inc(V.reciprocal(urec[:], u4[:]))
                        V.wait_ge(pUD.h, cdt)
                        V.drain()
                        for t in range(4):
                            inst = V.tensor_scalar(out=ctm[:, t, :], in0=ctp_ps[:, t, :],
                                                   scalar1=urec[:, t:t + 1], scalar2=None,
                                                   op0=ALU.mult)
                        cdm = pCD.inc(inst)
                        T.wait_ge(pCD.h, cdm)
                        T.wait_ge(pCT.h, pCT.n)
                        for t in range(4):
                            inst = T.transpose(ctf_ps[:, t, :], ctm[:, t, :], identB[:])
                        cdf = pUD.inc(inst)
                        V.wait_ge(pUD.h, cdf)
                        pCT.inc(V.tensor_copy(
                            ctxT[:, h, :].rearrange("p (a b) -> p a b", b=128),
                            ctf_ps[:, 0:4, :]))
                    ctxT_done = pCT.n
                    for osl in range(4):
                        if osl >= 2:
                            G.wait_ge(pY.h, wo_use[osl % 2])
                            for c in range(NC):
                                inst = G.dma_start(out=wo_sb[:, osl % 2, c, :],
                                                   in_=wo[osl, c * 128:(c + 1) * 128, :])
                                inst.then_inc(pWO.h, 16)
                                pWO.n += 16
                            wo_ev[osl] = pWO.n
                        for tb in range(4):
                            pr = tb % 2
                            T.wait_ge(pCT.h, ctxT_done)
                            T.wait_ge(pWO.h, wo_ev[osl])
                            T.wait_ge(pEV.h, ev_slot[pr])
                            if osl == 0 and tb in (0, 1):
                                T.wait_ge(pUR.h, pUR.n)
                            for c in range(NC):
                                inst = T.matmul(y_ps[:, pr, :],
                                                ctxT[:, c, tb * 128:(tb + 1) * 128],
                                                wo_sb[:, osl % 2, c, :],
                                                start=(c == 0), stop=(c == NC - 1))
                            y_at = pY.inc(inst)
                            wo_use[osl % 2] = y_at
                            V.wait_ge(pY.h, y_at)
                            if osl == 0:
                                V.wait_ge(pGE.h, pGE.n)
                            inst = V.scalar_tensor_tensor(
                                out=y4[:, tb, osl * 512:(osl + 1) * 512],
                                in0=y_ps[:, pr, :], scalar=1.0,
                                in1=bo_bc[:, osl * 512:(osl + 1) * 512],
                                op0=ALU.mult, op1=ALU.add)
                            ev_slot[pr] = pEV.inc(inst)
                    ev_all = pEV.n
                    for tb in range(4):
                        gt = q * 4 + tb
                        Y.wait_ge(pLN.h, pLN.n)
                        xo_at = pXO.inc(Y.dma_start(
                            out=xot[:], in_=x_own[gt * 128:(gt + 1) * 128, :]), 16)
                        S.wait_ge(pEV.h, ev_all)
                        S.wait_ge(pSP1.h, pSP1.n)
                        ge = pGE.inc(S.activation(out=e2[:], in_=y4[:, tb, :], func=AF.Gelu))
                        V.wait_ge(pGE.h, ge)
                        V.wait_ge(pXO.h, xo_at)
                        pLN.inc(V.tensor_scalar(out=y4[:, tb, :], in0=xot[:],
                                                scalar1=mu_all[:, gt:gt + 1],
                                                scalar2=rs_all[:, gt:gt + 1],
                                                op0=ALU.subtract, op1=ALU.mult))
                        V.drain()
                        V.tensor_add(y4[:, tb, :], y4[:, tb, :], e2[:])
                        V.drain()
                        emit_ln_stats(y4[:, tb, :], mu2[:], rs2[:], pLN, pLS)
                        V.wait_ge(pLS.h, pLS.n)
                        V.tensor_scalar(out=e2[:], in0=y4[:, tb, :], scalar1=mu2[:],
                                        scalar2=rs2[:], op0=ALU.subtract, op1=ALU.mult)
                        V.drain()
                        V.tensor_mul(e2[:], e2[:], gam_bc[:])
                        V.drain()
                        ls = pLS.inc(V.tensor_add(e2[:], e2[:], bet_bc[:]))
                        Y.wait_ge(pLS.h, ls)
                        Y.dma_start(out=s_scr[gt * 128:(gt + 1) * 128, :],
                                    in_=e2[:]).then_inc(pSP1.h, 16)
                        pSP1.n += 16
                Y.wait_ge(pSP1.h, pSP1.n)
                for eng in engines:
                    eng.nop().then_inc(p1_done.h, 1)
                p1_done.n = 5

            idx = [eng.compute_val(eng.partition_id() % 2) for eng in engines]
            saved = {id(s_): s_.n for s_ in p1_sems}
            for par in nc.Switch(engines=engines, index=idx, n=2):
                for s_ in p1_sems:
                    s_.n = saved[id(s_)]
                emit_attn(par)
            P1_DONE = 5
            p1d_h = p1_done.h

        kv_scope.close()

        engines = [T, V, S, G, Y]

        with nc.sbuf_tensor("xnT_f", [128, NC, TOK], BF16) as xnT_f, \
             nc.sbuf_tensor("wg_sb", [128, 2, NC, 256], BF16) as wg_sb, \
             nc.sbuf_tensor("si_sb", [128, 2, 512], BF16) as si_sb, \
             nc.sbuf_tensor("ht_sb", [128, 2, TOK], BF16) as ht_sb, \
             nc.psum_tensor("gps", [128, 2, 512], F32) as gps, \
             nc.psum_tensor("ups", [128, 2, 512], F32) as ups:

            cXN = sem("cXN"); cWG = sem("cWG"); cMM = sem("cMM")
            cSI = sem("cSI"); cHT = sem("cHT"); cSP = sem("cSP")

            for eng in engines:
                eng.wait_ge(p1d_h, P1_DONE)
            for q in range(4):
                cXN.inc(Y.dma_start(out=xnT_f[:, :, q * 512:(q + 1) * 512],
                                    in_=xn_scr[q]), 16)
            XN_N = cXN.n

            wg_ev = [0, 0]
            mm_at = []
            si_at = []
            ht_at = []
            sp_at = [0, 0]
            for j in range(64):
                jpr = j % 2
                if j >= 2:
                    G.wait_ge(cMM.h, mm_at[(j - 2) * 4 + 3])
                inst = G.dma_start(out=wg_sb[:, jpr, :, 0:128],
                                   in_=wgu[j].rearrange("(c p) n -> p c n", p=128))
                inst.then_inc(cWG.h, 16); cWG.n += 16
                inst = G.dma_start(out=wg_sb[:, jpr, :, 128:256],
                                   in_=wgu[j + 64].rearrange("(c p) n -> p c n", p=128))
                inst.then_inc(cWG.h, 16); cWG.n += 16
                wg_ev[jpr] = cWG.n
                for grp in range(4):
                    k = j * 4 + grp
                    gp = k % 2
                    T.wait_ge(cWG.h, wg_ev[jpr])
                    if k == 0:
                        T.wait_ge(cXN.h, XN_N)
                    if k >= 2:
                        T.wait_ge(cSI.h, si_at[k - 2])
                        T.wait_ge(cHT.h, ht_at[k - 2])
                    for c in range(NC):
                        T.matmul(gps[:, gp, :], wg_sb[:, jpr, c, 0:128],
                                 xnT_f[:, c, grp * 512:(grp + 1) * 512],
                                 start=(c == 0), stop=(c == NC - 1))
                    for c in range(NC):
                        inst = T.matmul(ups[:, gp, :], wg_sb[:, jpr, c, 128:256],
                                        xnT_f[:, c, grp * 512:(grp + 1) * 512],
                                        start=(c == 0), stop=(c == NC - 1))
                    mm_at.append(cMM.inc(inst))
                    S.wait_ge(cMM.h, mm_at[k])
                    si_at.append(cSI.inc(S.activation(out=si_sb[:, gp, :],
                                                      in_=gps[:, gp, :], func=AF.Silu)))
                    V.wait_ge(cSI.h, si_at[k])
                    if grp == 0 and j >= 2:
                        V.wait_ge(cSP.h, sp_at[jpr])
                    ht_at.append(cHT.inc(V.tensor_tensor(
                        ht_sb[:, jpr, grp * 512:(grp + 1) * 512],
                        si_sb[:, gp, :], ups[:, gp, :], ALU.mult)))
                Y.wait_ge(cHT.h, ht_at[-1])
                Y.dma_start(out=ht_scr[j], in_=ht_sb[:, jpr, :]).then_inc(cSP.h, 16)
                cSP.n += 16
                sp_at[jpr] = cSP.n
            P2_SP = cSP.n
            cSP_h = cSP.h

        with nc.sbuf_tensor("hT_f", [128, 64, 1024], BF16) as hT_f, \
             nc.sbuf_tensor("wd_sb", [128, 2, 4, 512], BF16) as wd_sb, \
             nc.sbuf_tensor("g_sb", [128, 2, 512], F32) as g_sb, \
             nc.sbuf_tensor("s_sb", [128, 2, 512], F32) as s_sb, \
             nc.sbuf_tensor("x_sb", [128, 2, 512], F32) as x_sb, \
             nc.sbuf_tensor("o_sb", [128, 2, 512], F32) as o_sb, \
             nc.psum_tensor("dps", [128, 8, 512], F32) as dps:

            dHT = sem("dHT"); dWD = sem("dWD"); dMM = sem("dMM"); dSI = sem("dSI")
            dSX = sem("dSX"); dOU = sem("dOU"); dEV = sem("dEV"); dCD = sem("dCD")

            for eng in engines:
                eng.wait_ge(cSP_h, P2_SP)

            ht_ev = [0] * 64
            ev_tile = {}
            sx_slot = [0, 0]
            ou_slot = [0, 0]
            g_slot = [0, 0]
            wd_use = [0, 0]
            for phi in range(2):
                for c in range(64):
                    if phi == 1:
                        Y.wait_ge(dCD.h, 1)
                    ht_ev[c] = dHT.inc(Y.dma_start(
                        out=hT_f[:, c, :],
                        in_=ht_scr[c][:, phi * 1024:(phi + 1) * 1024]), 16)
                for Ssl in range(4):
                    wd_ev = [0, 0]
                    last_mm = 0
                    for grp16 in range(16):
                        wpr = grp16 % 2
                        if wd_use[wpr]:
                            G.wait_ge(dMM.h, wd_use[wpr])
                        inst = G.dma_start(out=wd_sb[:, wpr, :, :],
                                           in_=wdn[Ssl, grp16 * 512:(grp16 + 1) * 512, :]
                                           .rearrange("(c p) n -> p c n", p=128))
                        inst.then_inc(dWD.h, 16); dWD.n += 16
                        wd_ev[wpr] = dWD.n
                        for c4 in range(4):
                            c = grp16 * 4 + c4
                            T.wait_ge(dWD.h, wd_ev[wpr])
                            T.wait_ge(dHT.h, ht_ev[c])
                            for tile in range(8):
                                if c == 0 and ev_tile.get(tile, 0):
                                    T.wait_ge(dEV.h, ev_tile[tile])
                                inst = T.matmul(dps[:, tile, :],
                                                hT_f[:, c, tile * 128:(tile + 1) * 128],
                                                wd_sb[:, wpr, c4, :],
                                                start=(c == 0), stop=(c == 63))
                            last_mm = dMM.inc(inst)
                        wd_use[wpr] = last_mm
                    for tile in range(8):
                        gt = phi * 8 + tile
                        tp = tile % 2
                        rows = slice(gt * 128, (gt + 1) * 128)
                        cols = slice(Ssl * 512, (Ssl + 1) * 512)
                        if sx_slot[tp]:
                            Y.wait_ge(dEV.h, sx_slot[tp])
                        dSX.inc(Y.dma_start(out=s_sb[:, tp, :], in_=s_scr[rows, cols]), 16)
                        sx_at = dSX.inc(Y.dma_start(out=x_sb[:, tp, :],
                                                    in_=x_own[rows, cols]), 16)
                        S.wait_ge(dMM.h, last_mm)
                        if g_slot[tp]:
                            S.wait_ge(dEV.h, g_slot[tp])
                        si2 = dSI.inc(S.activation(out=g_sb[:, tp, :], in_=dps[:, tile, :],
                                                   func=AF.Silu))
                        V.wait_ge(dSI.h, si2)
                        V.wait_ge(dSX.h, sx_at)
                        if ou_slot[tp]:
                            V.wait_ge(dOU.h, ou_slot[tp])
                        V.tensor_mul(o_sb[:, tp, :], s_sb[:, tp, :], g_sb[:, tp, :])
                        V.drain()
                        ev = dEV.inc(V.tensor_add(o_sb[:, tp, :], o_sb[:, tp, :],
                                                  x_sb[:, tp, :]))
                        ev_tile[tile] = ev
                        sx_slot[tp] = ev
                        g_slot[tp] = ev
                        Y.wait_ge(dEV.h, ev)
                        Y.dma_start(out=out[rows, cols],
                                    in_=o_sb[:, tp, :]).then_inc(dOU.h, 16)
                        dOU.n += 16
                        ou_slot[tp] = dOU.n
                if phi == 0:
                    T.nop().then_inc(dCD.h, 1)
                    dCD.n += 1
            Y.wait_ge(dOU.h, dOU.n)
    return nc


def _shard_inputs(inp):
    x = np.ascontiguousarray(inp["L0_attn_x"], dtype=np.float32)
    f32 = lambda a: np.asarray(a, dtype=np.float32)
    com = {
        "wq": np.ascontiguousarray(f32(inp["L0_attn_Wq"]).reshape(D, 4, 512).transpose(1, 0, 2)),
        "wk": np.ascontiguousarray(f32(inp["L0_attn_Wk"])),
        "wv": np.ascontiguousarray(f32(inp["L0_attn_Wv"])),
        "wo": np.ascontiguousarray(f32(inp["L0_attn_Wo"]).reshape(D, 4, 512).transpose(1, 0, 2)),
        "wgu": np.ascontiguousarray(f32(inp["L0_ffn_w_gate_up"]).T.reshape(D, 128, 128).transpose(1, 0, 2)),
        "wdn": np.ascontiguousarray(f32(inp["L0_ffn_w_down"]).T.reshape(INTER, 4, 512).transpose(1, 0, 2)),
        "bq": np.ascontiguousarray(f32(inp["L0_attn_bq"])[None, :]),
        "bk": np.ascontiguousarray(f32(inp["L0_attn_bk"])[None, :]),
        "bv": np.ascontiguousarray(f32(inp["L0_attn_bv"])[None, :]),
        "bo": np.ascontiguousarray(f32(inp["L0_attn_bo"])[None, :]),
        "gam": np.ascontiguousarray(f32(inp["L0_attn_gamma"])[None, :]),
        "bet": np.ascontiguousarray(f32(inp["L0_attn_beta"])[None, :]),
    }
    in_maps = []
    for c in range(8):
        b, half = c >> 1, c & 1
        xb = x[b]
        xo = xb[1024:3072] if half == 0 else np.concatenate([xb[0:1024], xb[3072:4096]], 0)
        m = dict(com)
        m["x_kv"] = np.ascontiguousarray(xb)
        m["x_own"] = np.ascontiguousarray(xo)
        in_maps.append(m)
    return in_maps


def _gather_output(results):
    outf = np.empty((B, L, D), np.float32)
    for c in range(8):
        b, half = c >> 1, c & 1
        o = results[c]["out"]
        if half == 0:
            outf[b, 1024:3072] = o
        else:
            outf[b, 0:1024] = o[0:1024]
            outf[b, 3072:4096] = o[1024:2048]
    return outf


def kernel(**inp):
    in_maps = _shard_inputs(inp)
    nc = build()
    res = run_bass_kernel_spmd(nc, in_maps, core_ids=list(range(8)))
    return _gather_output(res.results)


if __name__ == "__main__":
    nc = build()
    print("build ok")


# version 11
# speedup vs baseline: 1.1062x; 1.1062x over previous
import numpy as np
import ml_dtypes
import concourse.bass as bass
import concourse.mybir as mybir
from concourse.bass_utils import run_bass_kernel_spmd
from contextlib import ExitStack

F32 = mybir.dt.float32
BF16 = mybir.dt.bfloat16
AF = mybir.ActivationFunctionType
ALU = mybir.AluOpType
AX = mybir.AxisListType

B, L, D = 4, 4096, 2048
H, NKV, E = 16, 4, 128
INTER = 8192
TOK = 2048
NT = TOK // 128
NKT = L // 128
NC = D // 128
QSCALE = float(1.0 / np.sqrt(128.0))
EPS = 1e-5

R0_TABLE = {0: [1024, 1536, 2048, 2560], 1: [0, 512, 3072, 3584]}
OWN_G = {0: [2, 3, 4, 5], 1: [0, 1, 6, 7]}

DEBUG = False
SERIAL_P0 = False


class Sem:
    def __init__(self, nc, ctx, name):
        self.h = ctx.enter_context(nc.semaphore(name))
        self.n = 0

    def inc(self, inst, k=1):
        inst.then_inc(self.h, k)
        self.n += k
        return self.n


def build(debug=DEBUG):
    nc = bass.Bass("TRN2", target_bir_lowering=False)
    T, V, S, G, Y = nc.tensor, nc.vector, nc.scalar, nc.gpsimd, nc.sync

    dt_ = lambda n, s, d, k: nc.dram_tensor(n, s, d, kind=k).ap()
    x_kv = dt_("x_kv", [L, D], F32, "ExternalInput")
    x_own = dt_("x_own", [TOK, D], F32, "ExternalInput")
    wq = dt_("wq", [4, 128, NC, 512], F32, "ExternalInput")
    wk = dt_("wk", [128, NC, 512], F32, "ExternalInput")
    wv = dt_("wv", [128, NC, 512], F32, "ExternalInput")
    wo = dt_("wo", [4, 128, NC, 512], F32, "ExternalInput")
    wgu = dt_("wgu", [128, 128, NC, 128], F32, "ExternalInput")
    wdn = dt_("wdn", [4, 128, 64, 512], F32, "ExternalInput")
    bq = dt_("bq", [1, D], F32, "ExternalInput")
    bk = dt_("bk", [1, 512], F32, "ExternalInput")
    bv = dt_("bv", [1, 512], F32, "ExternalInput")
    bo = dt_("bo", [1, D], F32, "ExternalInput")
    gam = dt_("gam", [1, D], F32, "ExternalInput")
    bet = dt_("bet", [1, D], F32, "ExternalInput")
    out = dt_("out", [TOK, D], F32, "ExternalOutput")
    scr_kind = "ExternalOutput" if debug else "Internal"
    q_scr = dt_("q_scr", [4, 128, H, 512], BF16, scr_kind)
    xn_scr = dt_("xn_scr", [8, 128, NC, 512], BF16, scr_kind)
    xn_scr2 = dt_("xn_scr2", [4, 128, NC, 512], BF16, "Internal")
    ht_scr = dt_("ht_scr", [2, 128, 64, 1024], BF16, scr_kind)
    s_scr = dt_("s_scr", [TOK, D], F32, scr_kind)
    if debug:
        d_kt = dt_("d_kt", [128, NKV, L], BF16, "ExternalOutput")
        d_v = dt_("d_v", [128, NKT, 512], BF16, "ExternalOutput")
        d_xnt = dt_("d_xnt", [8, 128, NC, 512], BF16, "ExternalOutput")
        d_kt2 = dt_("d_kt2", [8, 128, NKV, 512], BF16, "ExternalOutput")
        d_wk = dt_("d_wk", [128, NC, 512], BF16, "ExternalOutput")

    ctx = ExitStack()
    with ctx:
        sbuf = lambda n, s, d: ctx.enter_context(nc.sbuf_tensor(n, s, d))
        sem = lambda n: Sem(nc, ctx, n)
        block = ctx.enter_context(nc.Block())

        mu_all = sbuf("mu_all", [128, NKT], F32)
        rs_all = sbuf("rs_all", [128, NKT], F32)
        identB = sbuf("identB", [128, 128], BF16)
        onesB = sbuf("onesB", [128, 1], BF16)
        cst = sbuf("cst", [128, 8], F32)
        bn_b = sbuf("bn_b", [128, 8, 3], F32)
        ln_t = sbuf("ln_t", [128, 12], F32)
        sq8 = sbuf("sq8", [128, 8], F32)

        sI = sem("sI")
        G.memset(identB[:], 0.0)
        sI.inc(G.affine_select(out=identB[:], in_=identB[:], compare_op=ALU.not_equal,
                               fill=1.0, base=0, pattern=[[-1, 128]], channel_multiplier=1))
        G.memset(onesB[:], 1.0)
        G.memset(cst[:, 0:1], EPS)
        G.memset(cst[:, 1:2], -0.5)
        G.memset(cst[:, 2:3], 0.125)
        G.memset(cst[:, 3:4], 1.0 / 2048.0)
        sI.inc(G.memset(cst[:, 4:5], QSCALE))
        INIT_N = sI.n

        def emit_ln_stats(xf_ap, mu_out, rstd_out, vsem, ssem, slot=0):
            o = slot
            for j in range(4):
                V.bn_stats(bn_b[:, 2 * j:2 * j + 2, :], xf_ap[:, j * 512:(j + 1) * 512])
            V.drain()
            V.reduce_sum(ln_t[:, o:o + 1], bn_b[:, :, 1], axis=AX.X)
            V.reduce_sum(ln_t[:, o + 1:o + 2], bn_b[:, :, 2], axis=AX.X)
            V.tensor_mul(sq8[:], bn_b[:, :, 1], bn_b[:, :, 1])
            V.drain()
            V.reduce_sum(ln_t[:, o + 3:o + 4], sq8[:], axis=AX.X)
            V.tensor_scalar_mul(mu_out, ln_t[:, o:o + 1], 0.125)
            V.drain()
            V.tensor_mul(ln_t[:, o + 4:o + 5], mu_out, mu_out)
            V.drain()
            V.tensor_scalar(out=ln_t[:, o + 4:o + 5], in0=ln_t[:, o + 3:o + 4],
                            scalar1=0.125, scalar2=ln_t[:, o + 4:o + 5],
                            op0=ALU.mult, op1=ALU.subtract)
            V.drain()
            v_ev = vsem.inc(V.tensor_scalar(out=ln_t[:, o + 1:o + 2],
                                            in0=ln_t[:, o + 1:o + 2],
                                            scalar1=1.0 / 2048.0,
                                            scalar2=ln_t[:, o + 4:o + 5],
                                            op0=ALU.mult, op1=ALU.add))
            S.wait_ge(vsem.h, v_ev)
            S.activation(out=ln_t[:, o + 2:o + 3], in_=ln_t[:, o + 1:o + 2],
                         func=AF.Ln, bias=cst[:, 0:1])
            S.drain()
            s_ev = ssem.inc(S.activation(out=rstd_out, in_=ln_t[:, o + 2:o + 3],
                                         func=AF.Exp, scale=cst[:, 1:2]))
            return s_ev

        kv_scope = ctx.enter_context(ExitStack())
        KT_sb = kv_scope.enter_context(nc.sbuf_tensor("KT", [128, NKV, L], BF16))
        V_sb = kv_scope.enter_context(nc.sbuf_tensor("Vt", [128, NKT, 512], BF16))

        with nc.sbuf_tensor("xbufA", [128, 3, D], F32) as xbuf, \
             nc.sbuf_tensor("xn_tmA", [128, 2, D], BF16) as xn_tm, \
             nc.sbuf_tensor("xnT_gA", [128, 2, NC, 512], BF16) as xnT_g, \
             nc.sbuf_tensor("wk_sb", [128, NC, 512], BF16) as wk_sb, \
             nc.sbuf_tensor("wv_sb", [128, NC, 512], BF16) as wv_sb, \
             nc.sbuf_tensor("bv_bc", [128, 512], F32) as bv_bc, \
             nc.sbuf_tensor("bk_pp", [128, NKV], F32) as bk_pp, \
             nc.psum_tensor("ptpA", [128, 4, 8, 128], BF16) as ptp, \
             nc.psum_tensor("kps", [128, 2, 512], F32) as kps, \
             nc.psum_tensor("vps", [128, 2, 512], F32) as vps:

            aX = sem("aX"); aR = sem("aR"); aN = sem("aN")
            aT = sem("aT"); aE = sem("aE")
            aKV = sem("aKV"); aKVE = sem("aKVE")
            aW = sem("aW"); aLV = sem("aLV"); aSPx = sem("aSPx")
            sp_ev_a = [0] * 8
            dXN = sem("dXN")

            aW.inc(Y.dma_start(out=bv_bc[:], in_=bv.partition_broadcast(128)), 16)
            aW.inc(Y.dma_start(out=bk_pp[:], in_=bk.rearrange("x (c p) -> (x p) c", p=128),
                               allow_slow_non_contiguous=True), 16)
            aW.inc(G.dma_start(out=wk_sb[:], in_=wk), 16)
            aW.inc(G.dma_start(out=wv_sb[:], in_=wv), 16)
            if debug:
                Y.wait_ge(aW.h, aW.n)
                aW.inc(Y.dma_start(out=d_wk, in_=wk_sb[:]), 16)
            AW_N = aW.n

            x_ev = [0] * NKT; n_ev = [0] * NKT; t_ev = [0] * NKT
            e_ev = [0] * NKT; xf_ev = [0] * NKT
            gu_ev = [0] * 8
            kve_ev = []
            kv_mm = []

            r_evs = [0] * NKT

            def emit_ln_tile(i):
                V.wait_ge(aX.h, x_ev[i])
                r_evs[i] = emit_ln_stats(xbuf[:, i % 3, :], mu_all[:, i:i + 1],
                                         rs_all[:, i:i + 1], aLV, aR,
                                         slot=(i % 2) * 6)

            x_ev[0] = aX.inc(Y.dma_start(out=xbuf[:, 0, :], in_=x_kv[0:128, :]), 16)
            for i in range(NKT):
                g, sl = i // 4, i % 3
                if i + 1 < NKT:
                    if i + 1 >= 3:
                        Y.wait_ge(aN.h, xf_ev[i - 2])
                    x_ev[i + 1] = aX.inc(Y.dma_start(
                        out=xbuf[:, (i + 1) % 3, :],
                        in_=x_kv[(i + 1) * 128:(i + 2) * 128, :]), 16)
                if i == 0:
                    V.wait_ge(sI.h, INIT_N)
                    emit_ln_tile(0)
                if i + 1 < NKT:
                    emit_ln_tile(i + 1)
                V.wait_ge(aR.h, r_evs[i])
                if i >= 2:
                    V.wait_ge(aT.h, t_ev[i - 2])
                inst = V.tensor_scalar(out=xn_tm[:, i % 2, :], in0=xbuf[:, sl, :],
                                       scalar1=mu_all[:, i:i + 1], scalar2=rs_all[:, i:i + 1],
                                       op0=ALU.subtract, op1=ALU.mult)
                n_ev[i] = aN.inc(inst)
                xf_ev[i] = n_ev[i]
                T.wait_ge(aN.h, n_ev[i])
                if i == 0:
                    T.wait_ge(sI.h, INIT_N)
                if i >= 2:
                    T.wait_ge(aE.h, e_ev[i - 2])
                for c in range(NC):
                    inst = T.transpose(ptp[:, (i % 2) * 2 + c // 8, c % 8, :],
                                       xn_tm[:, i % 2, c * 128:(c + 1) * 128], identB[:])
                t_ev[i] = aT.inc(inst)
                V.wait_ge(aT.h, t_ev[i])
                if g >= 2:
                    V.wait_ge(aKV.h, gu_ev[g - 2])
                    V.wait_ge(aSPx.h, sp_ev_a[g - 2])
                V.tensor_copy(xnT_g[:, g % 2, 0:8, (i % 4) * 128:(i % 4 + 1) * 128],
                              ptp[:, (i % 2) * 2, :, :])
                e_ev[i] = aE.inc(V.tensor_copy(
                    xnT_g[:, g % 2, 8:16, (i % 4) * 128:(i % 4 + 1) * 128],
                    ptp[:, (i % 2) * 2 + 1, :, :]))

                if i % 4 == 3:
                    Y.wait_ge(aE.h, e_ev[i])
                    Y.dma_start(out=xn_scr[g], in_=xnT_g[:, g % 2, :, :]).then_inc(aSPx.h, 16)
                    aSPx.n += 16
                    sp_ev_a[g] = aSPx.n
                    T.wait_ge(aE.h, e_ev[i])
                    if g == 0:
                        T.wait_ge(aW.h, AW_N)
                    for oc in range(NKV):
                        if SERIAL_P0 and kve_ev:
                            T.wait_ge(aKVE.h, kve_ev[-1])
                        elif len(kve_ev) >= 2:
                            T.wait_ge(aKVE.h, kve_ev[-2])
                        for c in range(NC):
                            inst = T.matmul(kps[:, oc % 2, :],
                                            wk_sb[:, c, oc * 128:(oc + 1) * 128],
                                            xnT_g[:, g % 2, c, :],
                                            start=(c == 0), stop=(c == NC - 1))
                        kv_mm.append(aKV.inc(inst))
                        V.wait_ge(aKV.h, kv_mm[-1])
                        kve_ev.append(aKVE.inc(V.tensor_scalar(
                            out=KT_sb[:, oc, g * 512:(g + 1) * 512], in0=kps[:, oc % 2, :],
                            scalar1=bk_pp[:, oc:oc + 1], scalar2=None, op0=ALU.add)))
                    if debug:
                        Y.wait_ge(aKVE.h, kve_ev[-1])
                        dXN.inc(Y.dma_start(out=d_kt2[g], in_=KT_sb[:, :, g * 512:(g + 1) * 512]), 16)
                    for t in range(4):
                        if SERIAL_P0 and kve_ev:
                            T.wait_ge(aKVE.h, kve_ev[-1])
                        elif len(kve_ev) >= 2:
                            T.wait_ge(aKVE.h, kve_ev[-2])
                        for c in range(NC):
                            inst = T.matmul(vps[:, t % 2, :],
                                            xnT_g[:, g % 2, c, t * 128:(t + 1) * 128],
                                            wv_sb[:, c, :],
                                            start=(c == 0), stop=(c == NC - 1))
                        kv_mm.append(aKV.inc(inst))
                        if t == 3:
                            gu_ev[g] = kv_mm[-1]
                        V.wait_ge(aKV.h, kv_mm[-1])
                        kve_ev.append(aKVE.inc(V.scalar_tensor_tensor(
                            out=V_sb[:, g * 4 + t, :], in0=vps[:, t % 2, :], scalar=1.0,
                            in1=bv_bc[:], op0=ALU.mult, op1=ALU.add)))

            P0A_V = aKVE.n
            aKVE_h = aKVE.h
            AKV_F = aKV.n
            aKV_h = aKV.h
            P0A_SPX = aSPx.n
            aSPx_h = aSPx.h

            if debug:
                Y.wait_ge(aKVE.h, P0A_V)
                Y.dma_start(out=d_kt, in_=KT_sb[:]).then_inc(sI.h, 16); sI.n += 16
                Y.dma_start(out=d_v, in_=V_sb[:]).then_inc(sI.h, 16); sI.n += 16

        with nc.sbuf_tensor("xnT_q", [128, 2, NC, 512], BF16) as xnT_q, \
             nc.sbuf_tensor("wq_sb", [128, 4, NC, 512], BF16) as wq_sb, \
             nc.sbuf_tensor("qt_evb", [128, 4, 512], BF16) as qt_evb, \
             nc.sbuf_tensor("bq_pp", [128, NC], F32) as bq_pp, \
             nc.psum_tensor("qps", [128, 4, 512], F32) as qps:

            bXN = sem("bXN"); bQ = sem("bQ"); bQE = sem("bQE")
            bW = sem("bW"); bSP = sem("bSP"); bX2 = sem("bX2")

            Y.wait_ge(aKVE_h, P0A_V)
            Y.wait_ge(aKV_h, AKV_F)
            G.wait_ge(aKVE_h, P0A_V)
            G.wait_ge(aKV_h, AKV_F)
            bW.inc(Y.dma_start(out=bq_pp[:], in_=bq.rearrange("x (c p) -> (x p) c", p=128),
                               allow_slow_non_contiguous=True), 16)
            for osl in range(4):
                bW.inc(G.dma_start(out=wq_sb[:, osl, :, :], in_=wq[osl]), 16)
            BW_N = bW.n

            qmm_ev = [0] * 4
            engines5 = [T, V, S, G, Y]
            yidx = [eng.compute_val(eng.partition_id() % 2) for eng in engines5]
            for par in nc.Switch(engines=engines5, index=yidx, n=2):
                for gl in range(4):
                    if gl >= 2:
                        Y.wait_ge(bQ.h, (gl - 1) * 4)
                        Y.wait_ge(bX2.h, (gl - 1) * 16)
                    Y.wait_ge(aSPx_h, P0A_SPX)
                    inst = Y.dma_start(out=xnT_q[:, gl % 2, :, :],
                                       in_=xn_scr[OWN_G[par][gl]])
                    inst.then_inc(bXN.h, 16)
            bXN.n = 64

            qe_ev = []
            spl = []
            for gl in range(4):
                T.wait_ge(bXN.h, (gl + 1) * 16)
                if gl == 0:
                    T.wait_ge(bW.h, BW_N)
                G.wait_ge(bXN.h, (gl + 1) * 16)
                G.dma_start(out=xn_scr2[gl],
                            in_=xnT_q[:, gl % 2, :, :]).then_inc(bX2.h, 16)
                bX2.n += 16
                for osl in range(4):
                    if qe_ev:
                        T.wait_ge(bQE.h, qe_ev[-1])
                    for oc in range(4):
                        for c in range(NC):
                            inst = T.matmul(qps[:, oc, :],
                                            wq_sb[:, osl, c, oc * 128:(oc + 1) * 128],
                                            xnT_q[:, gl % 2, c, :],
                                            start=(c == 0), stop=(c == NC - 1))
                    qmm_ev[gl] = bQ.inc(inst)
                    V.wait_ge(bQ.h, qmm_ev[gl])
                    if spl:
                        V.wait_ge(bSP.h, spl[-1])
                    for oc in range(4):
                        inst = V.tensor_scalar(out=qt_evb[:, oc, :], in0=qps[:, oc, :],
                                               scalar1=bq_pp[:, osl * 4 + oc:osl * 4 + oc + 1],
                                               scalar2=cst[:, 4:5],
                                               op0=ALU.add, op1=ALU.mult)
                    qe_ev.append(bQE.inc(inst))
                    G.wait_ge(bQE.h, qe_ev[-1])
                    G.dma_start(out=q_scr[gl][:, osl * 4:(osl + 1) * 4, :],
                                in_=qt_evb[:]).then_inc(bSP.h, 16)
                    bSP.n += 16
                    spl.append(bSP.n)

            P0B_SP = bSP.n
            bSP_h = bSP.h

        with ExitStack() as p1s:
            sb1 = lambda n, s, d: p1s.enter_context(nc.sbuf_tensor(n, s, d))
            ps1 = lambda n, s, d: p1s.enter_context(nc.psum_tensor(n, s, d))
            bo_bc = sb1("bo_bc", [128, D], F32)
            gam_bc = sb1("gam_bc", [128, D], F32)
            bet_bc = sb1("bet_bc", [128, D], F32)
            identF = sb1("identF", [128, 128], F32)
            mask_lt = sb1("mask_lt", [128, 128], F32)
            u_row = sb1("u_row", [128, 512], F32)
            qt_sb = sb1("qt_sb", [128, 2, 4, 512], BF16)
            wo_sb = sb1("wo_sb", [128, 2, NC, 512], BF16)
            PT_sb = sb1("PT_sb", [128, 2, 2, 512], BF16)
            u4 = sb1("u4", [128, 4], F32)
            urec = sb1("urec", [128, 4], F32)
            ctxu = sb1("ctxu", [128, 512], BF16)
            ctm = sb1("ctm", [128, 4, 128], BF16)
            ctxT = sb1("ctxT", [128, NC, 512], BF16)
            y4 = sb1("y4", [128, 4, D], F32)
            e2 = sb1("e2", [128, D], F32)
            xot = sb1("xot", [128, D], F32)
            mu2 = sb1("mu2", [128, 1], F32)
            rs2 = sb1("rs2", [128, 1], F32)

            sc_ps = ps1("sc_ps", [128, 2, 512], F32)
            cx_ps = ps1("cx_ps", [128, 2, 512], F32)
            u_ps = ps1("u_ps", [128, 2, 512], F32)
            ut_ps = ps1("ut_ps", [128, 512], F32)
            dn_ps = ps1("dn_ps", [128, 2, 4, 128], BF16)

            Y.wait_ge(bSP_h, P0B_SP)
            sI.inc(Y.dma_start(out=bo_bc[:], in_=bo.partition_broadcast(128)), 16)
            sI.inc(Y.dma_start(out=gam_bc[:], in_=gam.partition_broadcast(128)), 16)
            sI.inc(Y.dma_start(out=bet_bc[:], in_=bet.partition_broadcast(128)), 16)
            G.wait_ge(bSP_h, P0B_SP)
            G.memset(identF[:], 0.0)
            sI.inc(G.affine_select(out=identF[:], in_=identF[:], compare_op=ALU.not_equal,
                                   fill=1.0, base=0, pattern=[[-1, 128]],
                                   channel_multiplier=1))
            G.memset(mask_lt[:], 0.0)
            sI.inc(G.affine_select(out=mask_lt[:], in_=mask_lt[:], compare_op=ALU.is_ge,
                                   fill=-1e9, base=0, pattern=[[1, 128]],
                                   channel_multiplier=-1))
            sI.inc(G.memset(u_row[:], 0.0))
            LATE_N = sI.n

            pQT = sem("pQT"); pWO = sem("pWO"); pSC = sem("pSC"); pMK = sem("pMK")
            pPX = sem("pPX"); pCU = sem("pCU"); pCE = sem("pCE"); pUD = sem("pUD")
            pUR = sem("pUR"); pCD = sem("pCD"); pCT = sem("pCT")
            pY = sem("pY"); pEV = sem("pEV"); pGE = sem("pGE")
            pLN = sem("pLN"); pLS = sem("pLS"); pXO = sem("pXO"); pSP1 = sem("pSP1")
            p1_done = sem("p1_done")

            engines = [T, V, S, G, Y]
            p1_sems = [pQT, pWO, pSC, pMK, pPX, pCU, pCE, pUD, pUR, pCD, pCT,
                       pY, pEV, pGE, pLN, pLS, pXO, pSP1]

            def emit_attn(par):
                R0s = R0_TABLE[par]
                for eng in engines:
                    eng.wait_ge(aKVE_h, P0A_V)
                    eng.wait_ge(bSP_h, P0B_SP)
                    eng.wait_ge(sI.h, LATE_N)
                exp_slot = [0, 0]
                cu_slot = [[0, 0], [0, 0]]
                ev_slot = [0, 0]
                sc_end = [0]
                qtg_ev = {}
                wo_use = [0, 0]

                pending_epi = []

                def emit_epi_tile(q_, tb, ev_all):
                    gt = q_ * 4 + tb
                    gstat = R0s[q_] // 128 + tb
                    Y.wait_ge(pLN.h, pLN.n)
                    xo_at = pXO.inc(Y.dma_start(
                        out=xot[:], in_=x_own[gt * 128:(gt + 1) * 128, :]), 16)
                    S.wait_ge(pEV.h, ev_all)
                    S.wait_ge(pSP1.h, pSP1.n)
                    ge = pGE.inc(S.activation(out=e2[:], in_=y4[:, tb, :], func=AF.Gelu))
                    V.wait_ge(pGE.h, ge)
                    V.wait_ge(pXO.h, xo_at)
                    pLN.inc(V.tensor_scalar(out=y4[:, tb, :], in0=xot[:],
                                            scalar1=mu_all[:, gstat:gstat + 1],
                                            scalar2=rs_all[:, gstat:gstat + 1],
                                            op0=ALU.subtract, op1=ALU.mult))
                    V.drain()
                    V.tensor_add(y4[:, tb, :], y4[:, tb, :], e2[:])
                    V.drain()
                    emit_ln_stats(y4[:, tb, :], mu2[:], rs2[:], pLN, pLS)
                    V.wait_ge(pLS.h, pLS.n)
                    V.tensor_scalar(out=e2[:], in0=y4[:, tb, :], scalar1=mu2[:],
                                    scalar2=rs2[:], op0=ALU.subtract, op1=ALU.mult)
                    V.drain()
                    V.tensor_mul(e2[:], e2[:], gam_bc[:])
                    V.drain()
                    ls = pLS.inc(V.tensor_add(e2[:], e2[:], bet_bc[:]))
                    Y.wait_ge(pLS.h, ls)
                    Y.dma_start(out=s_scr[gt * 128:(gt + 1) * 128, :],
                                in_=e2[:]).then_inc(pSP1.h, 16)
                    pSP1.n += 16

                def load_qtg(q_, hg_, gate):
                    if gate:
                        Y.wait_ge(pSC.h, gate)
                    qtg_ev[(q_, hg_)] = pQT.inc(Y.dma_start(
                        out=qt_sb[:, hg_ % 2, :, :],
                        in_=q_scr[q_][:, hg_ * 4:(hg_ + 1) * 4, :]), 16)

                load_qtg(0, 0, 0)

                for q in range(4):
                    R0 = R0s[q]
                    KT0 = R0 // 128
                    ktmax = KT0 + 4
                    wo_ev = [0, 0, 0, 0]
                    for osl in range(2):
                        G.wait_ge(pY.h, wo_use[osl % 2])
                        inst = G.dma_start(out=wo_sb[:, osl % 2, :, :], in_=wo[osl])
                        inst.then_inc(pWO.h, 16)
                        pWO.n += 16
                        wo_ev[osl] = pWO.n

                    def emit_cu_pair(kt, ss, ktmax, ce_gate, kvp, expp):
                        for hp in (0, 1):
                            T.wait_ge(pPX.h, expp[hp][kt])
                            if kt == 0:
                                T.wait_ge(pCE.h, ce_gate)
                            T.matmul(cx_ps[:, hp, ss:512],
                                     V_sb[:, kt, kvp[hp] * 128:(kvp[hp] + 1) * 128],
                                     PT_sb[:, hp, kt % 2, ss:512],
                                     start=(kt == 0), stop=(kt == ktmax - 1))
                            inst = T.matmul(u_ps[0:1, hp, ss:512], onesB[:],
                                            PT_sb[:, hp, kt % 2, ss:512],
                                            start=(kt == 0), stop=(kt == ktmax - 1))
                            cu_slot[hp][kt % 2] = pCU.inc(inst)

                    for m in range(8):
                        hA, hB = 2 * m, 2 * m + 1
                        kvp = (hA % NKV, hB % NKV)
                        hg = hA // 4
                        if m % 2 == 0:
                            nq, nhg = (q, hg + 1) if hg < 3 else (q + 1, 0)
                            if nq < 4 and (q, hg) != (0, 0):
                                load_qtg(nq, nhg, sc_end[0])
                            elif (q, hg) == (0, 0):
                                load_qtg(0, 1, 0)
                        if m == 0:
                            T.wait_ge(pEV.h, pEV.n)
                        if pending_epi:
                            emit_epi_tile(*pending_epi.pop(0))
                        expp = ({}, {})
                        pend = None
                        ce_gate = pCE.n
                        cu_gate0 = [cu_slot[0][:], cu_slot[1][:]]
                        for kt in range(ktmax):
                            ss = 128 * max(0, kt - KT0)
                            for hp in (0, 1):
                                h = hA + hp
                                T.wait_ge(pPX.h, exp_slot[hp])
                                if m % 2 == 0 and kt == 0 and hp == 0:
                                    T.wait_ge(pQT.h, qtg_ev[(q, hg)])
                                inst = T.matmul(sc_ps[:, hp, ss:512],
                                                KT_sb[:, kvp[hp], kt * 128:(kt + 1) * 128],
                                                qt_sb[:, hg % 2, h % 4, ss:512],
                                                start=True, stop=True)
                                sc_at = pSC.inc(inst)
                                masked = kt >= KT0
                                if masked:
                                    V.wait_ge(pSC.h, sc_at)
                                    mk_at = pMK.inc(V.tensor_tensor(
                                        sc_ps[:, hp, ss:ss + 128],
                                        sc_ps[:, hp, ss:ss + 128],
                                        mask_lt[:], ALU.add))
                                S.wait_ge(pSC.h, sc_at)
                                if masked:
                                    S.wait_ge(pMK.h, mk_at)
                                S.wait_ge(pCU.h, cu_slot[hp][kt % 2])
                                expp[hp][kt] = pPX.inc(S.activation(
                                    out=PT_sb[:, hp, kt % 2, ss:512],
                                    in_=sc_ps[:, hp, ss:512], func=AF.Exp))
                                exp_slot[hp] = expp[hp][kt]
                            if pend is not None:
                                emit_cu_pair(pend[0], pend[1], ktmax, ce_gate, kvp, expp)
                            pend = (kt, ss)
                        emit_cu_pair(pend[0], pend[1], ktmax, ce_gate, kvp, expp)
                        sc_end[0] = pSC.n
                        for hp in (0, 1):
                            h = hA + hp
                            V.wait_ge(pCU.h, cu_slot[hp][0])
                            V.wait_ge(pCU.h, cu_slot[hp][1])
                            if hp == 1:
                                V.wait_ge(pUD.h, pUD.n)
                            V.tensor_copy(u_row[0:1, :], u_ps[0:1, hp, :])
                            ce1 = pCE.inc(V.tensor_copy(ctxu[:], cx_ps[:, hp, :]))
                            T.wait_ge(pCE.h, ce1)
                            T.wait_ge(pUR.h, pUR.n)
                            for t in range(4):
                                inst = T.transpose(ut_ps[:, t * 128:(t + 1) * 128],
                                                   u_row[:, t * 128:(t + 1) * 128],
                                                   identF[:])
                            ud = pUD.inc(inst)
                            T.wait_ge(pCD.h, pCD.n)
                            T.wait_ge(pCT.h, pCT.n)
                            for t in range(4):
                                inst = T.transpose(dn_ps[:, 0, t, :],
                                                   ctxu[:, t * 128:(t + 1) * 128], identB[:])
                            cdt = pUD.inc(inst)
                            V.wait_ge(pUD.h, ud)
                            V.tensor_copy(u4[:], ut_ps[:].rearrange(
                                "p (a b) -> p a b", b=128)[:, :, 0])
                            V.drain()
                            pUR.inc(V.reciprocal(urec[:], u4[:]))
                            V.wait_ge(pUD.h, cdt)
                            V.drain()
                            for t in range(4):
                                inst = V.tensor_scalar(out=ctm[:, t, :], in0=dn_ps[:, 0, t, :],
                                                       scalar1=urec[:, t:t + 1], scalar2=None,
                                                       op0=ALU.mult)
                            cdm = pCD.inc(inst)
                            T.wait_ge(pCD.h, cdm)
                            T.wait_ge(pCT.h, pCT.n)
                            for t in range(4):
                                inst = T.transpose(dn_ps[:, 1, t, :], ctm[:, t, :], identB[:])
                            cdf = pUD.inc(inst)
                            V.wait_ge(pUD.h, cdf)
                            pCT.inc(V.tensor_copy(
                                ctxT[:, h, :].rearrange("p (a b) -> p a b", b=128),
                                dn_ps[:, 1, :, :]))
                    ctxT_done = pCT.n
                    ce_all = pCE.n
                    for osl in range(4):
                        if osl >= 2:
                            G.wait_ge(pY.h, wo_use[osl % 2])
                            inst = G.dma_start(out=wo_sb[:, osl % 2, :, :], in_=wo[osl])
                            inst.then_inc(pWO.h, 16)
                            pWO.n += 16
                            wo_ev[osl] = pWO.n
                        for tb in range(4):
                            pr = tb % 2
                            T.wait_ge(pCT.h, ctxT_done)
                            T.wait_ge(pWO.h, wo_ev[osl])
                            T.wait_ge(pEV.h, ev_slot[pr])
                            if osl == 0 and tb in (0, 1):
                                T.wait_ge(pCE.h, ce_all)
                                T.wait_ge(pCU.h, pCU.n)
                            for c in range(NC):
                                inst = T.matmul(cx_ps[:, pr, :],
                                                ctxT[:, c, tb * 128:(tb + 1) * 128],
                                                wo_sb[:, osl % 2, c, :],
                                                start=(c == 0), stop=(c == NC - 1))
                            y_at = pY.inc(inst)
                            wo_use[osl % 2] = y_at
                            V.wait_ge(pY.h, y_at)
                            if osl == 0:
                                V.wait_ge(pGE.h, pGE.n)
                            inst = V.scalar_tensor_tensor(
                                out=y4[:, tb, osl * 512:(osl + 1) * 512],
                                in0=cx_ps[:, pr, :], scalar=1.0,
                                in1=bo_bc[:, osl * 512:(osl + 1) * 512],
                                op0=ALU.mult, op1=ALU.add)
                            ev_slot[pr] = pEV.inc(inst)
                    ev_all = pEV.n
                    for tb in range(4):
                        pending_epi.append((q, tb, ev_all))
                while pending_epi:
                    emit_epi_tile(*pending_epi.pop(0))
                Y.wait_ge(pSP1.h, pSP1.n)
                for eng in engines:
                    eng.nop().then_inc(p1_done.h, 1)
                p1_done.n = 5

            idx = [eng.compute_val(eng.partition_id() % 2) for eng in engines]
            saved = {id(s_): s_.n for s_ in p1_sems}
            for par in nc.Switch(engines=engines, index=idx, n=2):
                for s_ in p1_sems:
                    s_.n = saved[id(s_)]
                emit_attn(par)
            P1_DONE = 5
            p1d_h = p1_done.h

        kv_scope.close()

        engines = [T, V, S, G, Y]

        with nc.sbuf_tensor("xnT_f", [128, 4, NC, 512], BF16) as xnT_f, \
             nc.sbuf_tensor("wg_sb", [128, 2, NC, 256], BF16) as wg_sb, \
             nc.sbuf_tensor("si_sb", [128, 2, 512], BF16) as si_sb, \
             nc.sbuf_tensor("ht_sb", [128, 2, TOK], BF16) as ht_sb, \
             nc.psum_tensor("gps", [128, 2, 512], F32) as gps, \
             nc.psum_tensor("ups", [128, 2, 512], F32) as ups:

            cXN = sem("cXN"); cWG = sem("cWG"); cMM = sem("cMM")
            cSI = sem("cSI"); cHT = sem("cHT"); cSP = sem("cSP")

            for eng in engines:
                eng.wait_ge(p1d_h, P1_DONE)
            for q in range(4):
                cXN.inc(Y.dma_start(out=xnT_f[:, q, :, :], in_=xn_scr2[q]), 16)
            XN_N = cXN.n

            wg_ev = [0, 0]
            mm_at = []
            si_at = []
            ht_at = []
            sp_at = [0, 0]
            for j in range(64):
                jpr = j % 2
                if j >= 2:
                    G.wait_ge(cMM.h, mm_at[(j - 2) * 4 + 3])
                inst = G.dma_start(out=wg_sb[:, jpr, :, 0:128], in_=wgu[j])
                inst.then_inc(cWG.h, 16); cWG.n += 16
                inst = G.dma_start(out=wg_sb[:, jpr, :, 128:256], in_=wgu[j + 64])
                inst.then_inc(cWG.h, 16); cWG.n += 16
                wg_ev[jpr] = cWG.n
                for grp in range(4):
                    k = j * 4 + grp
                    gp = k % 2
                    T.wait_ge(cWG.h, wg_ev[jpr])
                    if k == 0:
                        T.wait_ge(cXN.h, XN_N)
                    if k >= 2:
                        T.wait_ge(cSI.h, si_at[k - 2])
                        T.wait_ge(cHT.h, ht_at[k - 2])
                    for c in range(NC):
                        T.matmul(gps[:, gp, :], wg_sb[:, jpr, c, 0:128],
                                 xnT_f[:, grp, c, :],
                                 start=(c == 0), stop=(c == NC - 1))
                    for c in range(NC):
                        inst = T.matmul(ups[:, gp, :], wg_sb[:, jpr, c, 128:256],
                                        xnT_f[:, grp, c, :],
                                        start=(c == 0), stop=(c == NC - 1))
                    mm_at.append(cMM.inc(inst))
                    S.wait_ge(cMM.h, mm_at[k])
                    si_at.append(cSI.inc(S.activation(out=si_sb[:, gp, :],
                                                      in_=gps[:, gp, :], func=AF.Silu)))
                    V.wait_ge(cSI.h, si_at[k])
                    if grp == 0 and j >= 2:
                        V.wait_ge(cSP.h, sp_at[jpr])
                    ht_at.append(cHT.inc(V.tensor_tensor(
                        ht_sb[:, jpr, grp * 512:(grp + 1) * 512],
                        si_sb[:, gp, :], ups[:, gp, :], ALU.mult)))
                Y.wait_ge(cHT.h, ht_at[-1])
                Y.dma_start(out=ht_scr[0][:, j, :],
                            in_=ht_sb[:, jpr, 0:1024]).then_inc(cSP.h, 16)
                cSP.n += 16
                Y.dma_start(out=ht_scr[1][:, j, :],
                            in_=ht_sb[:, jpr, 1024:2048]).then_inc(cSP.h, 16)
                cSP.n += 16
                sp_at[jpr] = cSP.n
            P2_SP = cSP.n
            cSP_h = cSP.h

        with nc.sbuf_tensor("hT_f", [128, 64, 1024], BF16) as hT_f, \
             nc.sbuf_tensor("wd_sb", [128, 2, 4, 512], BF16) as wd_sb, \
             nc.sbuf_tensor("g_sb", [128, 2, 512], F32) as g_sb, \
             nc.sbuf_tensor("s_sb", [128, 2, 512], F32) as s_sb, \
             nc.sbuf_tensor("x_sb", [128, 2, 512], F32) as x_sb, \
             nc.sbuf_tensor("o_sb", [128, 2, 512], F32) as o_sb, \
             nc.psum_tensor("dps", [128, 8, 512], F32) as dps:

            dHT = sem("dHT"); dWD = sem("dWD"); dMM = sem("dMM"); dSI = sem("dSI")
            dSX = sem("dSX"); dOU = sem("dOU"); dEV = sem("dEV"); dCD = sem("dCD")

            for eng in engines:
                eng.wait_ge(cSP_h, P2_SP)

            ht_ev = [0] * 64
            ev_tile = {}
            sx_slot = [0, 0]
            ou_slot = [0, 0]
            g_slot = [0, 0]
            wd_use = [0, 0]
            ssl3_mm = [0] * 16
            for phi in range(2):
                for cg in range(4):
                    if phi == 1:
                        Y.wait_ge(dMM.h, ssl3_mm[cg * 4 + 3])
                    ev = dHT.inc(Y.dma_start(
                        out=hT_f[:, cg * 16:(cg + 1) * 16, :],
                        in_=ht_scr[phi][:, cg * 16:(cg + 1) * 16, :]), 16)
                    for c in range(cg * 16, (cg + 1) * 16):
                        ht_ev[c] = ev
                for Ssl in range(4):
                    wd_ev = [0, 0]
                    last_mm = 0
                    tile_mm = [0] * 8
                    for grp16 in range(16):
                        wpr = grp16 % 2
                        if wd_use[wpr]:
                            G.wait_ge(dMM.h, wd_use[wpr])
                        inst = G.dma_start(out=wd_sb[:, wpr, :, :],
                                           in_=wdn[Ssl][:, grp16 * 4:(grp16 + 1) * 4, :])
                        inst.then_inc(dWD.h, 16); dWD.n += 16
                        wd_ev[wpr] = dWD.n
                        for c4 in range(4):
                            c = grp16 * 4 + c4
                            T.wait_ge(dWD.h, wd_ev[wpr])
                            T.wait_ge(dHT.h, ht_ev[c])
                            for tile in range(8):
                                if c == 0 and ev_tile.get(tile, 0):
                                    T.wait_ge(dEV.h, ev_tile[tile])
                                inst = T.matmul(dps[:, tile, :],
                                                hT_f[:, c, tile * 128:(tile + 1) * 128],
                                                wd_sb[:, wpr, c4, :],
                                                start=(c == 0), stop=(c == 63))
                                if c == 63:
                                    tile_mm[tile] = dMM.inc(inst)
                            if c != 63:
                                last_mm = dMM.inc(inst)
                        wd_use[wpr] = dMM.n
                        if Ssl == 3:
                            ssl3_mm[grp16] = dMM.n
                    for tile in range(8):
                        gt = phi * 8 + tile
                        tp = tile % 2
                        rows = slice(gt * 128, (gt + 1) * 128)
                        cols = slice(Ssl * 512, (Ssl + 1) * 512)
                        if sx_slot[tp]:
                            Y.wait_ge(dEV.h, sx_slot[tp])
                        dSX.inc(Y.dma_start(out=s_sb[:, tp, :], in_=s_scr[rows, cols]), 16)
                        sx_at = dSX.inc(Y.dma_start(out=x_sb[:, tp, :],
                                                    in_=x_own[rows, cols]), 16)
                        S.wait_ge(dMM.h, tile_mm[tile])
                        if g_slot[tp]:
                            S.wait_ge(dEV.h, g_slot[tp])
                        si2 = dSI.inc(S.activation(out=g_sb[:, tp, :], in_=dps[:, tile, :],
                                                   func=AF.Silu))
                        V.wait_ge(dSI.h, si2)
                        V.wait_ge(dSX.h, sx_at)
                        if ou_slot[tp]:
                            V.wait_ge(dOU.h, ou_slot[tp])
                        V.tensor_mul(o_sb[:, tp, :], s_sb[:, tp, :], g_sb[:, tp, :])
                        V.drain()
                        ev = dEV.inc(V.tensor_add(o_sb[:, tp, :], o_sb[:, tp, :],
                                                  x_sb[:, tp, :]))
                        ev_tile[tile] = ev
                        sx_slot[tp] = ev
                        g_slot[tp] = ev
                        Y.wait_ge(dEV.h, ev)
                        Y.dma_start(out=out[rows, cols],
                                    in_=o_sb[:, tp, :]).then_inc(dOU.h, 16)
                        dOU.n += 16
                        ou_slot[tp] = dOU.n
                if phi == 0:
                    T.nop().then_inc(dCD.h, 1)
                    dCD.n += 1
            Y.wait_ge(dOU.h, dOU.n)
    return nc


def _shard_inputs(inp):
    x = np.ascontiguousarray(inp["L0_attn_x"], dtype=np.float32)
    f32 = lambda a: np.asarray(a, dtype=np.float32)
    com = {
        "wq": np.ascontiguousarray(f32(inp["L0_attn_Wq"]).reshape(NC, 128, 4, 512).transpose(2, 1, 0, 3)),
        "wk": np.ascontiguousarray(f32(inp["L0_attn_Wk"]).reshape(NC, 128, 512).transpose(1, 0, 2)),
        "wv": np.ascontiguousarray(f32(inp["L0_attn_Wv"]).reshape(NC, 128, 512).transpose(1, 0, 2)),
        "wo": np.ascontiguousarray(f32(inp["L0_attn_Wo"]).reshape(NC, 128, 4, 512).transpose(2, 1, 0, 3)),
        "wgu": np.ascontiguousarray(f32(inp["L0_ffn_w_gate_up"]).T.reshape(NC, 128, 128, 128).transpose(2, 1, 0, 3)),
        "wdn": np.ascontiguousarray(f32(inp["L0_ffn_w_down"]).T.reshape(64, 128, 4, 512).transpose(2, 1, 0, 3)),
        "bq": np.ascontiguousarray(f32(inp["L0_attn_bq"])[None, :]),
        "bk": np.ascontiguousarray(f32(inp["L0_attn_bk"])[None, :]),
        "bv": np.ascontiguousarray(f32(inp["L0_attn_bv"])[None, :]),
        "bo": np.ascontiguousarray(f32(inp["L0_attn_bo"])[None, :]),
        "gam": np.ascontiguousarray(f32(inp["L0_attn_gamma"])[None, :]),
        "bet": np.ascontiguousarray(f32(inp["L0_attn_beta"])[None, :]),
    }
    in_maps = []
    for c in range(8):
        b, half = c >> 1, c & 1
        xb = x[b]
        xo = xb[1024:3072] if half == 0 else np.concatenate([xb[0:1024], xb[3072:4096]], 0)
        m = dict(com)
        m["x_kv"] = np.ascontiguousarray(xb)
        m["x_own"] = np.ascontiguousarray(xo)
        in_maps.append(m)
    return in_maps


def _gather_output(results):
    outf = np.empty((B, L, D), np.float32)
    for c in range(8):
        b, half = c >> 1, c & 1
        o = results[c]["out"]
        if half == 0:
            outf[b, 1024:3072] = o
        else:
            outf[b, 0:1024] = o[0:1024]
            outf[b, 3072:4096] = o[1024:2048]
    return outf


def kernel(**inp):
    in_maps = _shard_inputs(inp)
    nc = build()
    res = run_bass_kernel_spmd(nc, in_maps, core_ids=list(range(8)))
    return _gather_output(res.results)


if __name__ == "__main__":
    nc = build()
    print("build ok")
